# version 01 (final)
# baseline (speedup 1.0000x reference)
import numpy as np
import ml_dtypes
from contextlib import ExitStack
import concourse.bass as bass
import concourse.mybir as mybir
from concourse.bass_utils import run_bass_kernel_spmd

F32 = mybir.dt.float32
BF16 = mybir.dt.bfloat16
I32 = mybir.dt.int32
AF = mybir.ActivationFunctionType
OP = mybir.AluOpType
AX = mybir.AxisListType
BF = ml_dtypes.bfloat16

T = 2048
D = 4096
I = 11008
KT = D // 128
MT = I // 128
NTT = T // 128
HALF = 1024
K_KEEP = I // 2
EPS = 1e-5

N_SUB_IT = 13
N_FULL_IT = 9
SUB_STRIDE = 8
SUB_N = I // SUB_STRIDE


def chunk_list():
    out = []
    m = 0
    while m < MT:
        n = min(4, MT - m)
        out.append((m, n))
        m += n
    return out

CHL = chunk_list()
NCH = len(CHL)


class S:
    def __init__(self, h):
        self.h = h
        self.n = 0


class Prog:

    def __init__(self, nc):
        self.nc = nc
        self.q = {"t": [], "v": [], "s": [], "y": [], "g": []}

    def emit(self, eng, fn):
        self.q[eng].append(fn)

    def wait(self, eng, sem, val):
        self.q[eng].append(lambda e, s=sem.h, v=val: e.wait_ge(s, v))

    def op(self, eng, fn, inc=None, k=1):
        if inc is not None:
            inc.n += k
            self.q[eng].append(lambda e, f=fn, s=inc.h, kk=k: f(e).then_inc(s, kk))
        else:
            self.q[eng].append(fn)


def build_nc(phases=("p0", "a", "b1", "b2", "b3", "c", "f"), debug_outputs=(),
             nch_lim=None):
    nch_use = NCH if nch_lim is None else min(nch_lim, NCH)
    nc = bass.Bass("TRN2", target_bir_lowering=False)
    es = ExitStack()

    x_in = nc.dram_tensor("x", [T, D], F32, kind="ExternalInput").ap()
    wg_in = nc.dram_tensor("w_gate", [I, D], F32, kind="ExternalInput").ap()
    wu_in = nc.dram_tensor("w_up", [I, D], F32, kind="ExternalInput").ap()
    wd_in = nc.dram_tensor("w_down", [D, I], F32, kind="ExternalInput").ap()
    wf_in = nc.dram_tensor("w_ffn", [D, D], F32, kind="ExternalInput").ap()
    bias_in = nc.dram_tensor("bias", [1, D], F32, kind="ExternalInput").ap()
    lA_in = nc.dram_tensor("lora_A", [128, KT * 48], F32, kind="ExternalInput").ap()
    lB_in = nc.dram_tensor("lora_B", [48, D], F32, kind="ExternalInput").ap()
    ssi_in = nc.dram_tensor("ssi", [1, 4], I32, kind="ExternalInput").ap()
    emat_in = nc.dram_tensor("emat", [48, 3], F32, kind="ExternalInput").ap()
    ident_in = nc.dram_tensor("ident", [128, 128], F32, kind="ExternalInput").ap()
    out_hbm = nc.dram_tensor("out", [T, D], F32, kind="ExternalOutput").ap()

    wg_bf = nc.dram_tensor("wg_bf", [I, D], BF16, kind="Internal").ap()
    wu_bf = nc.dram_tensor("wu_bf", [I, D], BF16, kind="Internal").ap()
    wd_bf = nc.dram_tensor("wd_bf", [D, I], BF16, kind="Internal").ap()
    wf_bf = nc.dram_tensor("wf_bf", [D, D], BF16, kind="Internal").ap()
    xn_scr = nc.dram_tensor("xn_scr", [T, D], BF16, kind="Internal").ap()
    aT_scr = nc.dram_tensor("aT_scr", [I, T], BF16, kind="Internal").ap()
    aL_scr = nc.dram_tensor("aL_scr", [I, T], BF16, kind="Internal").ap()
    qT_scr = nc.dram_tensor("qT_scr", [I, T], BF16, kind="Internal").ap()
    pT_scr = nc.dram_tensor("pT_scr", [8, MT, 128, 256], BF16, kind="Internal").ap()
    y1_scr = nc.dram_tensor("y1_scr", [T, D], F32, kind="Internal").ap()
    xp_scr = nc.dram_tensor("xp_scr", [T, D], F32, kind="Internal").ap()
    xn2_scr = nc.dram_tensor("xn2_scr", [T, D], BF16, kind="Internal").ap()
    y2_scr = nc.dram_tensor("y2_scr", [T, D], F32, kind="Internal").ap()

    dbg = {}
    dbg_map = {"aT": (aT_scr, [I, T], BF16), "qT": (qT_scr, [I, T], BF16),
               "aL": (aL_scr, [I, T], BF16),
               "pT": (pT_scr.rearrange("r k p t -> (r k p) t"), [8 * MT * 128, 256], BF16), "xn": (xn_scr, [T, D], BF16),
               "y1": (y1_scr, [T, D], F32), "xp": (xp_scr, [T, D], F32),
               "xn2": (xn2_scr, [T, D], BF16), "y2": (y2_scr, [T, D], F32)}
    for name in debug_outputs:
        if name not in dbg_map:
            continue
        src, shp, dt = dbg_map[name]
        dbg[name] = (nc.dram_tensor("dbg_" + name, shp, dt, kind="ExternalOutput").ap(), src)
    tdbg = None
    if "tcol" in debug_outputs:
        tdbg = nc.dram_tensor("dbg_tcol", [128, 128], F32, kind="ExternalOutput").ap()
    prdbg = None
    if "pairdump" in debug_outputs:
        prdbg = nc.dram_tensor("dbg_pair", [128, MT * 256], BF16, kind="ExternalOutput").ap()
    tmdbg = None
    if "tmat" in debug_outputs:
        tmdbg = nc.dram_tensor("dbg_tmat", [128, T], F32, kind="ExternalOutput").ap()
    e2dbg = None
    if "e2" in debug_outputs:
        e2dbg = nc.dram_tensor("dbg_e2", [128, 2 * NTT], F32, kind="ExternalOutput").ap()

    A = nc.alloc_sbuf_tensor_at
    smalls = A("smalls", [128, 256], F32, offset=17408)
    t_mat = A("t_mat", [128, T], BF16, offset=18432)
    onesf = A("onesf", [1, 128], F32, offset=22592)
    onesb = A("onesb", [1, 128], BF16, offset=23104)
    trow = A("trow", [1, 128], F32, offset=23168)
    t_stage = A("t_stage", [128, 1], F32, offset=23680)
    t_tile = A("t_tile", [128, NTT], F32, offset=23744)
    ssi_i = A("ssi_i", [1, 4], I32, offset=23872)
    ssi_f = A("ssi_f", [1, 4], F32, offset=23936)
    gb = A("gb", [1, 8], F32, offset=24000)
    g3 = A("g3", [3, 1], F32, offset=24064)
    g3b = A("g3b", [3, 1], BF16, offset=24128)
    d48 = A("d48", [48, 1], F32, offset=24192)
    ones48 = A("ones48", [1, 48], F32, offset=24256)
    emat_f = A("emat_f", [48, 3], F32, offset=24384)
    ident_sb = A("ident_sb", [128, 128], F32, offset=24704)
    xnT = (A("xnT", [128, KT * 1024], BF16, offset=25600) if "a" in phases else None)
    g_ring = (A("g_ring", [128, KT * 512], BF16, offset=91136) if "a" in phases else None)
    u_ring = (A("u_ring", [128, KT * 512], BF16, offset=123904) if "a" in phases else None)
    s_stg = (A("s_stg", [128, 8 * 512], F32, offset=156672) if "a" in phases else None)
    u_stg = (A("u_stg", [128, 4 * 512], F32, offset=173056) if "a" in phases else None)
    a_stg = (A("a_stg", [128, 8 * 512], BF16, offset=181248) if "a" in phases else None)
    q_stg = (A("q_stg", [128, 8 * 512], BF16, offset=189440) if "a" in phases else None)
    _p0_shift = 0 if "a" in phases else 0
    if "a" in phases:
        x_t0 = A("x_t0", [128, D], F32, offset=91136)
        x_t1 = A("x_t1", [128, D], F32, offset=107520)
        sqdump = A("sqdump", [128, D], F32, offset=123904)
        xn_b0 = A("xn_b0", [128, D], BF16, offset=140288)
        xn_b1 = A("xn_b1", [128, D], BF16, offset=148480)
    else:
        x_t0 = A("x_t0", [128, D], F32, offset=91136)
        x_t1 = A("x_t1", [128, D], F32, offset=107520)
        sqdump = A("sqdump", [128, D], F32, offset=123904)
        xn_b0 = A("xn_b0", [128, D], BF16, offset=140288)
        xn_b1 = A("xn_b1", [128, D], BF16, offset=148480)
    a_full = A("a_full", [128, I], BF16, offset=25600)
    m1_aT = A("m1_aT", [128, 1024], BF16, offset=219712)
    m1_qT = A("m1_qT", [128, 1024], BF16, offset=221760)
    m1_p = A("m1_p", [128, 1024], BF16, offset=223808)
    wd_stack = A("wd_stack", [128, MT * 512], BF16, offset=25600)
    pairs = [A("pair0", [128, MT * 256], BF16, offset=113664),
             A("pair1", [128, MT * 256], BF16, offset=179712)]
    y1_stg = A("y1_stg", [128, 2 * 512], F32, offset=181248)
    y1_tile = A("y1_tile", [128, D], F32, offset=113664)
    x_t2 = A("x_t2", [128, D], F32, offset=130048)
    xp_tile = A("xp_tile", [128, D], F32, offset=146432)
    xn2_tile = A("xn2_tile", [128, D], BF16, offset=162816)
    xn2T = A("xn2T", [128, KT * 1024], BF16, offset=25600)
    wf_stack = A("wf_stack", [128, KT * 1024], BF16, offset=91136)
    AcatT = A("AcatT", [128, KT * 48], BF16, offset=171008)
    BcatT = A("BcatT", [48, D], BF16, offset=174144)
    dmidT = A("dmidT", [48, T], BF16, offset=156736)
    bias_b = A("bias_b", [128, D], BF16, offset=182336)
    bias_row = A("bias_row", [1, D], F32, offset=209920)
    y2_stg = A("y2_stg", [128, 2 * 512], F32, offset=211008)
    bisdump = A("bisdump", [128, I], BF16, offset=179712)
    a_full1 = A("a_full1", [128, I], BF16, offset=47616)
    aL_full = A("aL_full", [128, I], BF16, offset=69632)
    arec = [A(f"arec{j}", [128, I], F32, offset=91648 + j * 44032) for j in range(2)]
    t_mat_f = A("t_mat_f", [128, T], F32, offset=201728)
    af_stg = A("af_stg", [128, 4 * 512], F32, offset=197632)
    alo_stg = A("alo_stg", [128, 8 * 512], BF16, offset=205824)
    mk_hi = [A(f"mkhi{j}", [128, 1024], BF16, offset=25600 + j * 2048) for j in range(2)]
    mk_lo = [A(f"mklo{j}", [128, 1024], BF16, offset=29696 + j * 2048) for j in range(2)]
    mk_re = [A(f"mkre{j}", [128, 1024], F32, offset=33792 + j * 4096) for j in range(2)]
    mk_q = [A(f"mkq{j}", [128, 1024], BF16, offset=41984 + j * 2048) for j in range(2)]
    ln_tile = A("ln_tile", [128, D], F32, offset=153600)
    ln_dump = A("ln_dump", [128, D], F32, offset=190528)
    tmp48 = A("tmp48", [48, 3], F32, offset=24512)
    d48dmp = A("d48dmp", [48, 3], F32, offset=24832)
    gw = A("gw", [1, 16], F32, offset=24960)
    y2_tile = A("y2_tile", [128, D], F32, offset=113664)
    xp2_tile = A("xp2_tile", [128, D], F32, offset=130048)
    out_tile = A("out_tile", [128, D], F32, offset=146432)

    pb = [nc.alloc_psum_tensor(f"pb{i}", [128, 512], F32) for i in range(8)]

    sem_names = ["castg", "castu", "castd", "castf", "castl",
                 "xload", "sq", "rstd", "vst", "xnsp", "xnt",
                 "secg", "secu", "psA", "drA", "ucp", "qrd", "absd", "aspill", "qspill",
                 "atr", "bis", "bisv", "tmm", "tmat", "mrd", "pTsp",
                 "stk", "pairld", "psB", "drB", "y1sp",
                 "y1rd", "b3v", "b3r", "xpsp", "xn2sp",
                 "x2t", "wfstk", "lora", "psC", "drC", "y2sp",
                 "y2rd", "fv", "fr", "outsp", "misc", "maskv", "psL", "alov", "alosp", "arecv"]
    sm = {n: S(es.enter_context(nc.semaphore(n))) for n in sem_names}
    P = Prog(nc)

    if "a" in phases:
        for c, (mb, nm) in enumerate(CHL[:nch_use]):
            r0, r1 = mb * 128, (mb + nm) * 128
            P.op("g", lambda g, r0=r0, r1=r1: g.dma_start(out=wg_bf[r0:r1, :], in_=wg_in[r0:r1, :]),
                 inc=sm["castg"], k=16)
            P.op("g", lambda g, r0=r0, r1=r1: g.dma_start(out=wu_bf[r0:r1, :], in_=wu_in[r0:r1, :]),
                 inc=sm["castu"], k=16)
    if "b2" in phases:
        for s_ in range(8):
            P.op("g", lambda g, s_=s_: g.dma_start(out=wd_bf[s_ * 512:(s_ + 1) * 512, :],
                                                   in_=wd_in[s_ * 512:(s_ + 1) * 512, :]),
                 inc=sm["castd"], k=16)
    if "c" in phases:
        for qq in range(4):
            P.op("g", lambda g, qq=qq: g.dma_start(out=wf_bf[qq * 1024:(qq + 1) * 1024, :],
                                                   in_=wf_in[qq * 1024:(qq + 1) * 1024, :]),
                 inc=sm["castf"], k=16)

    if "noconst" not in phases:
        P.op("y", lambda y: y.dma_start(out=ident_sb[:, :], in_=ident_in), inc=sm["misc"], k=16)
        P.op("y", lambda y: y.dma_start(out=emat_f[:, :], in_=emat_in), inc=sm["misc"], k=16)
        P.op("y", lambda y: y.dma_start(out=ssi_i[:, :], in_=ssi_in), inc=sm["misc"], k=16)

    if "p0" in phases:
        P.op("v", lambda v: v.memset(smalls[:, 16:17], EPS))
        P.op("v", lambda v: v.memset(smalls[:, 17:18], 64.0))
        P.op("v", lambda v: v.memset(smalls[:, 18:19], 1e-3))
        P.op("v", lambda v: v.memset(smalls[:, 27:28], 0.0))
        P.op("v", lambda v: v.memset(smalls[:, 30:32], 0.0))
        P.op("v", lambda v: v.memset(onesb[:, :], 1.0))
        P.op("v", lambda v: v.memset(onesf[:, :], 1.0), inc=sm["vst"], k=1)
        for tt in range(NTT):
            xt = x_t0 if tt % 2 == 0 else x_t1
            xnb = xn_b0 if tt % 2 == 0 else xn_b1
            base = 8 * (tt % 2)
            s1 = smalls[:, base + 0:base + 1]
            m = smalls[:, base + 1:base + 2]
            ss = smalls[:, base + 2:base + 3]
            e2 = smalls[:, base + 3:base + 4]
            nmsq = smalls[:, base + 4:base + 5]
            negm = smalls[:, base + 5:base + 6]
            rec = smalls[:, base + 6:base + 7]
            rstd = smalls[:, base + 7:base + 8]
            eps_c = smalls[:, 16:17]
            if tt >= 2:
                P.wait("y", sm["rstd"], tt - 1)
            P.op("y", lambda y, xt=xt, tt=tt: y.dma_start(out=xt[:, :], in_=x_in[tt * 128:(tt + 1) * 128, :]),
                 inc=sm["xload"], k=16)
            P.wait("s", sm["xload"], sm["xload"].n)
            if tt == 0:
                P.wait("s", sm["vst"], 1)
            if tt >= 2:
                P.wait("s", sm["xnsp"], 16 * (tt - 1))
            P.op("s", lambda s, xt=xt, s1=s1: s.activation(out=sqdump[:, :], in_=xt[:, :],
                                                           func=AF.Identity, accum_out=s1))
            P.op("s", lambda s, m=m, s1=s1: s.mul(m, s1, 1.0 / D))
            P.op("s", lambda s, xt=xt, ss=ss: s.activation(out=sqdump[:, :], in_=xt[:, :],
                                                           func=AF.Square, accum_out=ss))
            P.op("s", lambda s, nmsq=nmsq, m=m: s.activation(out=nmsq, in_=m, func=AF.Square))
            P.op("s", lambda s, nmsq=nmsq, eps_c=eps_c: s.activation(out=nmsq, in_=nmsq,
                                                                     func=AF.Identity, scale=-1.0,
                                                                     bias=eps_c))
            P.op("s", lambda s, e2=e2, ss=ss, nmsq=nmsq: s.activation(out=e2, in_=ss,
                                                                      func=AF.Identity,
                                                                      scale=1.0 / D, bias=nmsq))
            P.op("s", lambda s, rec=rec, e2=e2: s.activation(out=rec, in_=e2, func=AF.Ln))
            P.op("s", lambda s, rstd=rstd, rec=rec: s.activation(out=rstd, in_=rec, func=AF.Exp,
                                                                 scale=-0.5))
            P.op("s", lambda s, negm=negm, m=m: s.mul(negm, m, -1.0), inc=sm["sq"], k=1)
            P.wait("s", sm["sq"], sm["sq"].n)
            P.op("s", lambda s, xt=xt, negm=negm: s.activation(out=sqdump[:, :], in_=xt[:, :],
                                                               func=AF.Identity, bias=negm))
            P.op("s", lambda s, xnb=xnb, rstd=rstd: s.activation(out=xnb[:, :], in_=sqdump[:, :],
                                                                 func=AF.Identity, scale=rstd),
                 inc=sm["rstd"], k=1)
            P.wait("y", sm["rstd"], sm["rstd"].n)
            if "nospill" in phases:
                sm["xnsp"].n += 16
                P.op("y", lambda y, xnb=xnb, tt=tt: y.dma_start(out=xn_scr[tt * 128:(tt + 1) * 128, 0:64],
                                                                in_=xnb[0:128, 0:64]).then_inc(sm["xnsp"].h, 16))
            else:
                P.op("y", lambda y, xnb=xnb, tt=tt: y.dma_start(out=xn_scr[tt * 128:(tt + 1) * 128, :],
                                                                in_=xnb[:, :]),
                     inc=sm["xnsp"], k=16)
            if e2dbg is not None:
                P.wait("v", sm["rstd"], sm["rstd"].n)
                P.op("v", lambda v, e2=e2, tt=tt: v.tensor_copy(out=smalls[:, 32 + tt:33 + tt], in_=e2))
                P.op("v", lambda v, rstd=rstd, tt=tt: v.tensor_copy(out=smalls[:, 64 + tt:65 + tt], in_=rstd))

    swA = {"n": 0}
    pend_qsp = []

    def flush_qspills():
        for fn in pend_qsp:
            fn()
        pend_qsp.clear()

    def emit_phase_a():
        for h in range(2):
            P.wait("y", sm["xnsp"], 16 * 8 * (h + 1))
            if h == 1:
                P.wait("y", sm["psA"], swA["n"])
            for k in range(KT):
                P.op("y", lambda y, k=k, h=h: y.dma_start_transpose(
                    xnT[:, k * 1024:(k + 1) * 1024],
                    xn_scr[h * HALF:(h + 1) * HALF, k * 128:(k + 1) * 128]),
                    inc=sm["xnt"], k=16)
            P.wait("t", sm["xnt"], sm["xnt"].n)
            for c, (mb, nm) in enumerate(CHL[:nch_use]):
                emit_chunk(h, c, mb, nm)
        flush_qspills()

    def emit_chunk(h, c, mb, nm):
        cseq = h * nch_use + c
        P.wait("y", sm["castg"], 16 * (c + 1))
        if cseq == 0:
            P.wait("y", sm["xnsp"], 16 * NTT)
        else:
            P.wait("y", sm["psA"], prev_last_g[0])
        for k in range(KT):
            P.op("y", lambda y, k=k, mb=mb, nm=nm: y.dma_start_transpose(
                g_ring[:, k * 512:k * 512 + nm * 128],
                wg_bf[mb * 128:(mb + nm) * 128, k * 128:(k + 1) * 128]),
                inc=sm["secg"], k=16)
        P.wait("y", sm["castu"], 16 * (c + 1))
        if cseq == 0:
            P.wait("y", sm["xnsp"], 16 * NTT)
        else:
            P.wait("y", sm["psA"], prev_last_u[0])
        for k in range(KT):
            P.op("y", lambda y, k=k, mb=mb, nm=nm: y.dma_start_transpose(
                u_ring[:, k * 512:k * 512 + nm * 128],
                wu_bf[mb * 128:(mb + nm) * 128, k * 128:(k + 1) * 128]),
                inc=sm["secu"], k=16)
        flush_qspills()

        secg_goal = sm["secg"].n
        secu_goal = sm["secu"].n

        pair_ids = [(0, 1)] if nm == 2 else [(0, 1), (2, 3)]
        for pi, (ma, mb_l) in enumerate(pair_ids):
            for mat in ("g", "u"):
                bankset = 0 if mat == "g" else 4
                ring = g_ring if mat == "g" else u_ring
                P.wait("t", sm["secg" if mat == "g" else "secu"],
                       secg_goal if mat == "g" else secu_goal)
                need = drained_upto(bankset)
                if need > 0:
                    P.wait("t", sm["drA"], need)
                for k in range(KT):
                    for j, m_l in enumerate((ma, mb_l)):
                        for tb in range(2):
                            bank = bankset + 2 * j + tb
                            last = (k == KT - 1 and j == 1 and tb == 1)
                            fn = (lambda t, bank=bank, ring=ring, k=k, m_l=m_l, tb=tb, kk=KT:
                                  t.matmul(pb[bank][:, :],
                                           ring[:, k * 512 + m_l * 128:k * 512 + (m_l + 1) * 128],
                                           xnT[:, k * 1024 + tb * 512:k * 1024 + (tb + 1) * 512],
                                           start=(k == 0), stop=(k == kk - 1),
                                           skip_group_check=True))
                            if last:
                                P.op("t", fn, inc=sm["psA"], k=1)
                            else:
                                P.op("t", fn)
                swA["n"] = sm["psA"].n
                record_sweep(mat, pi, bankset)
                if mat == "g":
                    emit_g_drain(h, c, mb, ma, mb_l, pi, bankset)
                else:
                    emit_u_drain(h, c, mb, ma, mb_l, pi, bankset)

    sweep_hist = []
    prev_last_g = [0]
    prev_last_u = [0]
    bank_drains = {0: [], 4: []}

    def drained_upto(bankset):
        lst = bank_drains[bankset]
        return lst[-1] if lst else 0

    def record_sweep(mat, pi, bankset):
        if mat == "g":
            prev_last_g[0] = sm["psA"].n
        else:
            prev_last_u[0] = sm["psA"].n

    def emit_g_drain(h, c, mb, ma, mb_l, pi, bankset):
        P.wait("s", sm["psA"], sm["psA"].n)
        pp = pi % 2
        for j, m_l in enumerate((ma, mb_l)):
            for tb in range(2):
                bank = bankset + 2 * j + tb
                slot = 2 * j + tb
                sslot = pp * 4 + slot
                if squse_hist[sslot] > 0:
                    P.wait("s", sm["qrd"], squse_hist[sslot])
                P.op("s", lambda s, bank=bank, sslot=sslot:
                     s.activation(out=s_stg[:, sslot * 512:(sslot + 1) * 512],
                                  in_=pb[bank][:, :], func=AF.Silu),
                     inc=sm["drA"], k=1)
                if aspill_hist[sslot] > 0:
                    P.wait("s", sm["aspill"], aspill_hist[sslot])
                if afuse_hist[slot] > 0:
                    P.wait("s", sm["alov"], afuse_hist[slot])
                P.op("s", lambda s, sslot=sslot, slot=slot:
                     s.activation(out=af_stg[:, slot * 512:(slot + 1) * 512],
                                  in_=s_stg[:, sslot * 512:(sslot + 1) * 512], func=AF.Abs))
                P.op("s", lambda s, sslot=sslot, slot=slot:
                     s.copy(out=a_stg[:, sslot * 512:(sslot + 1) * 512],
                            in_=af_stg[:, slot * 512:(slot + 1) * 512]),
                     inc=sm["absd"], k=1)
                m_g = mb + m_l
                P.wait("s", sm["absd"], sm["absd"].n)
                P.op("s", lambda s, m_g=m_g, sslot=sslot, h=h, tb=tb:
                     s.dma_start(out=aT_scr[m_g * 128:(m_g + 1) * 128,
                                            h * HALF + tb * 512:h * HALF + (tb + 1) * 512],
                                 in_=a_stg[:, sslot * 512:(sslot + 1) * 512]),
                     inc=sm["aspill"], k=16)
                aspill_hist[sslot] = sm["aspill"].n
                P.wait("v", sm["absd"], sm["absd"].n)
                if alosp_hist[sslot] > 0:
                    P.wait("v", sm["alosp"], alosp_hist[sslot])
                P.op("v", lambda v, sslot=sslot, slot=slot:
                     v.scalar_tensor_tensor(out=alo_stg[:, sslot * 512:(sslot + 1) * 512],
                                            in0=af_stg[:, slot * 512:(slot + 1) * 512],
                                            scalar=1.0,
                                            in1=a_stg[:, sslot * 512:(sslot + 1) * 512],
                                            op0=OP.mult, op1=OP.subtract),
                     inc=sm["alov"], k=1)
                afuse_hist[slot] = sm["alov"].n
                alov_goal = sm["alov"].n
                sm["alosp"].n += 16
                alosp_hist[sslot] = sm["alosp"].n

                def alosp(m_g=m_g, sslot=sslot, h=h, tb=tb, alov_goal=alov_goal):
                    P.wait("y", sm["alov"], alov_goal)
                    P.emit("y", lambda y: y.dma_start(
                        out=aL_scr[m_g * 128:(m_g + 1) * 128,
                                   h * HALF + tb * 512:h * HALF + (tb + 1) * 512],
                        in_=alo_stg[:, sslot * 512:(sslot + 1) * 512]).then_inc(sm["alosp"].h, 16))
                pend_qsp.append(alosp)
                silu_hist[sslot] = sm["drA"].n
        bank_drains[bankset].append(sm["drA"].n)

    def emit_u_drain(h, c, mb, ma, mb_l, pi, bankset):
        P.wait("s", sm["psA"], sm["psA"].n)
        pp = pi % 2
        ucopy_counts = []
        slots = []
        for j, m_l in enumerate((ma, mb_l)):
            for tb in range(2):
                bank = bankset + 2 * j + tb
                slot = 2 * j + tb
                slots.append((j, tb, m_l, slot, pp * 4 + slot, bank))
                if ustg_hist[slot] > 0:
                    P.wait("s", sm["qrd"], ustg_hist[slot])
                P.op("s", lambda s, bank=bank, slot=slot:
                     s.copy(out=u_stg[:, slot * 512:(slot + 1) * 512], in_=pb[bank][:, :]),
                     inc=sm["drA"], k=1)
                ucopy_counts.append(sm["drA"].n)
        P.op("s", lambda s: s.mul(smalls[:, 30:31], smalls[:, 30:31], 1.0), inc=sm["drA"], k=1)
        settle_n = sm["drA"].n
        for idx, (j, tb, m_l, slot, sslot, bank) in enumerate(slots):
            lag_n = ucopy_counts[idx + 1] if idx + 1 < len(ucopy_counts) else settle_n
            P.wait("v", sm["drA"], lag_n)
            if qspill_hist[sslot] > 0:
                P.wait("v", sm["qspill"], qspill_hist[sslot])
            P.op("v", lambda v, sslot=sslot, slot=slot:
                 v.tensor_mul(q_stg[:, sslot * 512:(sslot + 1) * 512],
                              s_stg[:, sslot * 512:(sslot + 1) * 512],
                              u_stg[:, slot * 512:(slot + 1) * 512]),
                 inc=sm["qrd"], k=1)
            ustg_hist[slot] = sm["qrd"].n
            squse_hist[sslot] = sm["qrd"].n
            m_g = mb + m_l
            qrd_goal = sm["qrd"].n
            sm["qspill"].n += 16
            qspill_hist[sslot] = sm["qspill"].n

            def qsp(m_g=m_g, sslot=sslot, h=h, tb=tb, qrd_goal=qrd_goal):
                P.wait("y", sm["qrd"], qrd_goal)
                P.emit("y", lambda y: y.dma_start(
                    out=qT_scr[m_g * 128:(m_g + 1) * 128,
                               h * HALF + tb * 512:h * HALF + (tb + 1) * 512],
                    in_=q_stg[:, sslot * 512:(sslot + 1) * 512]).then_inc(sm["qspill"].h, 16))
            pend_qsp.append(qsp)
        bank_drains[bankset].append(sm["drA"].n)

    squse_hist = [0] * 8
    silu_hist = [0] * 8
    afuse_hist = [0] * 4
    alosp_hist = [0] * 8
    ustg_hist = [0] * 4
    aspill_hist = [0] * 8
    qspill_hist = [0] * 8

    if "a" in phases:
        emit_phase_a()
    aspill_total = sm["aspill"].n
    qspill_total = sm["qspill"].n
    alosp_total = sm["alosp"].n

    NSUB = 14
    NFULL = 12
    C_NLO = 20
    C_W0 = 21
    C_W1 = 22
    C_S = 23
    C_SG = 24
    C_G = 25
    C_NG = 26
    C_SPARE = 27
    C_POS = 28
    C_HI0 = 17
    C_TINY = 18

    def b1_iter(it, sub, abuf, trace=False):
        par = it % 2
        wsrc = smalls[:, C_W0 + par:C_W0 + par + 1]
        wdst = smalls[:, C_W0 + (1 - par):C_W0 + (2 - par)]
        nlo = smalls[:, C_NLO:C_NLO + 1]
        S = smalls[:, C_S:C_S + 1]
        sg = smalls[:, C_SG:C_SG + 1]
        g = smalls[:, C_G:C_G + 1]
        ng = smalls[:, C_NG:C_NG + 1]
        spare = smalls[:, C_SPARE:C_SPARE + 1]
        nmid = smalls[:, C_POS:C_POS + 1]
        P.op("s", lambda s, wdst=wdst, wsrc=wsrc: s.mul(wdst, wsrc, 0.5), inc=sm["bisv"], k=1)
        P.wait("s", sm["bisv"], sm["bisv"].n)
        P.op("s", lambda s, nmid=nmid, wdst=wdst, nlo=nlo: s.activation(
            out=nmid, in_=wdst, func=AF.Identity, scale=-1.0, bias=nlo), inc=sm["bisv"], k=1)
        P.wait("s", sm["bisv"], sm["bisv"].n)
        if sub:
            a_ap = abuf[:, 0:I:SUB_STRIDE]
            d_ap = bisdump[:, 0:I:SUB_STRIDE]
        else:
            a_ap = abuf[:, :]
            d_ap = bisdump[:, :]
        P.op("s", lambda s, a_ap=a_ap, d_ap=d_ap, nmid=nmid, S=S: s.activation(
            out=d_ap, in_=a_ap, func=AF.Sign, bias=nmid, scale=1.0, accum_out=S),
            inc=sm["bisv"], k=1)
        P.wait("s", sm["bisv"], sm["bisv"].n)
        P.op("s", lambda s, spare=spare: s.mul(spare, spare, 0.5))
        P.op("s", lambda s, sg=sg, S=S: s.activation(out=sg, in_=S, func=AF.Sign))
        P.op("s", lambda s, g=g, sg=sg: s.activation(out=g, in_=sg, func=AF.Relu))
        P.op("s", lambda s, ng=ng, g=g: s.mul(ng, g, -1.0), inc=sm["bisv"], k=1)
        P.wait("s", sm["bisv"], sm["bisv"].n)
        P.op("s", lambda s, nlo=nlo, wdst=wdst, ng=ng: s.activation(
            out=nlo, in_=wdst, func=AF.Identity, scale=ng, bias=nlo), inc=sm["bisv"], k=1)
        P.wait("s", sm["bisv"], sm["bisv"].n)
        if trace:
            P.op("s", lambda s, it=it, S=S: s.mul(smalls[:, 32 + 2 * it:33 + 2 * it], S, 1.0))
            P.op("s", lambda s, it=it, nlo=nlo: s.mul(smalls[:, 33 + 2 * it:34 + 2 * it], nlo, 1.0))


    def pe_settle(bank, inc, k=1):
        P.op("t", lambda t, bank=bank: t.matmul(pb[bank][0:1, 0:8], onesb[0:1, 0:1],
                                                onesb[0:1, 0:8], start=True, stop=True,
                                                skip_group_check=True), inc=inc, k=k)
    def emit_b1_tau(tau):
        abuf = a_full if tau % 2 == 0 else a_full1
        P.wait("y", sm["aspill"], aspill_total)
        P.wait("y", sm["alosp"], alosp_total)
        if tau > 1:
            P.wait("y", sm["arecv"], tau - 1)
        P.op("y", lambda y, tau=tau, abuf=abuf: y.dma_start_transpose(
            abuf[:, :], aT_scr[0:I, tau * 128:(tau + 1) * 128]), inc=sm["atr"], k=16)
        if tau > 0:
            P.wait("y", sm["arecv"], tau)
        P.op("y", lambda y, tau=tau: y.dma_start_transpose(
            aL_full[:, :], aL_scr[0:I, tau * 128:(tau + 1) * 128]), inc=sm["atr"], k=16)

    def emit_b1_bisect(tau):
        abuf = a_full if tau % 2 == 0 else a_full1
        ar = arec[tau % 2]
        P.wait("v", sm["atr"], 32 * (tau + 1))
        if tau == 0:
            for _ in range(3):
                P.op("v", lambda v: v.tensor_copy(out=bisdump[:, :], in_=abuf[:, :]))
        if tau >= 1:
            P.wait("v", sm["bis"], tau)
        P.op("v", lambda v, ar=ar, abuf=abuf: v.tensor_add(ar[:, :], abuf[:, :], aL_full[:, :]),
             inc=sm["arecv"], k=1)
        P.wait("s", sm["arecv"], sm["arecv"].n)
        P.op("s", lambda s, ar=ar: s.activation(out=bisdump[:, :], in_=ar[:, :],
                                                func=AF.Sign,
                                                accum_out=smalls[:, C_SPARE:C_SPARE + 1]))
        nlo = smalls[:, C_NLO:C_NLO + 1]
        eps_c = smalls[:, 16:17]
        hi_c = smalls[:, C_HI0:C_HI0 + 1]
        tiny_c = smalls[:, C_TINY:C_TINY + 1]
        P.op("s", lambda s, nlo=nlo, eps_c=eps_c: s.activation(out=nlo, in_=eps_c,
                                                               func=AF.Identity, scale=0.0))
        P.op("s", lambda s, hi_c=hi_c: s.copy(out=smalls[:, C_W0:C_W0 + 1], in_=hi_c),
             inc=sm["bisv"], k=1)
        P.wait("s", sm["bisv"], sm["bisv"].n)
        for it in range(NSUB):
            b1_iter(it, sub=True, abuf=ar, trace=(tau == 0 and tdbg is not None))
        par = NSUB % 2
        P.op("s", lambda s, nlo=nlo: s.activation(
            out=smalls[:, C_W0 + par:C_W0 + par + 1], in_=nlo, func=AF.Identity,
            scale=-1.35, bias=tiny_c))
        P.op("s", lambda s, nlo=nlo: s.mul(nlo, nlo, 0.55), inc=sm["bisv"], k=1)
        P.wait("s", sm["bisv"], sm["bisv"].n)
        for it in range(NFULL):
            b1_iter(NSUB + it, sub=False, abuf=ar, trace=(tau == 0 and tdbg is not None))
        parf = (NSUB + NFULL) % 2
        P.op("s", lambda s, nlo=nlo: s.mul(smalls[:, C_POS:C_POS + 1], nlo, -1.0),
             inc=sm["bisv"], k=1)
        P.wait("s", sm["bisv"], sm["bisv"].n)
        P.op("s", lambda s, tau=tau, parf=parf: s.activation(
            out=t_tile[:, tau:tau + 1], in_=smalls[:, C_W0 + parf:C_W0 + parf + 1],
            func=AF.Identity, scale=1.0, bias=smalls[:, C_POS:C_POS + 1]),
            inc=sm["bis"], k=1)

    def emit_bcast_tau(tau):
        P.op("s", lambda s, tau=tau: s.copy(out=t_stage[:, :], in_=t_tile[:, tau:tau + 1]))
        P.op("s", lambda s: s.mul(smalls[0:1, C_SPARE:C_SPARE + 1],
                                  smalls[0:1, C_SPARE:C_SPARE + 1], 1.0), inc=sm["tmm"], k=1)
        P.wait("t", sm["tmm"], sm["tmm"].n)
        P.op("t", lambda t: t.matmul(pb[6][0:1, 0:128], t_stage[:, :],
                                     ident_sb[:, :], start=True, stop=True))
        pe_settle(5, sm["tmm"])

        P.wait("s", sm["tmm"], sm["tmm"].n)
        P.op("s", lambda s: s.copy(out=trow[:, :], in_=pb[6][0:1, 0:128]))
        P.op("s", lambda s: s.mul(smalls[0:1, C_SPARE:C_SPARE + 1],
                                  smalls[0:1, C_SPARE:C_SPARE + 1], 1.0), inc=sm["tmm"], k=1)
        P.wait("t", sm["tmm"], sm["tmm"].n)
        P.op("t", lambda t: t.matmul(pb[7][:, 0:128], onesf[:, :], trow[:, :],
                                     start=True, stop=True))
        pe_settle(5, sm["tmm"])
        P.wait("v", sm["tmm"], sm["tmm"].n)
        P.op("v", lambda v, tau=tau: v.tensor_copy(out=t_mat_f[:, tau * 128:(tau + 1) * 128],
                                                   in_=pb[7][:, 0:128]), inc=sm["tmat"], k=1)

    def emit_mask(h):
        P.wait("v", sm["tmat"], sm["tmat"].n)
        for k in range(MT):
            sl = k % 2
            at = mk_hi[sl]
            lt = mk_lo[sl]
            rt = mk_re[sl]
            qt = mk_q[sl]
            cols = slice(h * HALF, (h + 1) * HALF)
            if k >= 2:
                P.wait("y", sm["pTsp"], sm["pTsp"].n - 16)
            if k == 0:
                P.wait("y", sm["qspill"], qspill_total)
                P.wait("y", sm["aspill"], aspill_total)
                P.wait("y", sm["alosp"], alosp_total)
                P.wait("y", sm["bis"], NTT)
            P.op("y", lambda y, at=at, k=k, cols=cols: y.dma_start(
                out=at[:, :], in_=aT_scr[k * 128:(k + 1) * 128, cols]), inc=sm["mrd"], k=16)
            P.op("y", lambda y, lt=lt, k=k, cols=cols: y.dma_start(
                out=lt[:, :], in_=aL_scr[k * 128:(k + 1) * 128, cols]), inc=sm["mrd"], k=16)
            P.op("y", lambda y, qt=qt, k=k, cols=cols: y.dma_start(
                out=qt[:, :], in_=qT_scr[k * 128:(k + 1) * 128, cols]), inc=sm["mrd"], k=16)
            P.wait("v", sm["mrd"], sm["mrd"].n)
            P.op("v", lambda v, rt=rt, at=at, lt=lt: v.tensor_add(rt[:, :], at[:, :], lt[:, :]))
            P.op("v", lambda v, rt=rt, at=at, h=h: v.tensor_tensor(
                out=at[:, :], in0=rt[:, :], in1=t_mat_f[:, h * HALF:(h + 1) * HALF], op=OP.is_gt))
            P.op("v", lambda v, at=at, qt=qt: v.tensor_mul(qt[:, :], at[:, :], qt[:, :]),
                 inc=sm["maskv"], k=1)
            P.wait("y", sm["maskv"], sm["maskv"].n)
            P.op("y", lambda y, qt=qt, k=k, h=h: y.dma_start(
                out=pT_scr[4 * h:4 * h + 4, k, :, :].rearrange("r p t -> p r t"),
                in_=qt.rearrange("p (r t) -> p r t", r=4)), inc=sm["pTsp"], k=16)

    if "b1" in phases:
        emit_b1_tau(0)
        for tau in range(NTT):
            if tau + 1 < NTT:
                emit_b1_tau(tau + 1)
            emit_b1_bisect(tau)
            emit_bcast_tau(tau)
        for h in range(2):
            emit_mask(h)
    pTsp_total = sm["pTsp"].n

    drB_hist = [0, 0]
    if "b2" in phases:
        for slab in range(8):
            P.wait("y", sm["castd"], 16 * (slab + 1))
            if slab > 0:
                P.wait("y", sm["psB"], sm["psB"].n)
            elif "a" in phases:
                P.wait("y", sm["psA"], sm["psA"].n)
            for k in range(MT):
                P.op("y", lambda y, k=k, slab=slab: y.dma_start_transpose(
                    wd_stack[:, k * 512:(k + 1) * 512],
                    wd_bf[slab * 512:(slab + 1) * 512, k * 128:(k + 1) * 128]),
                    inc=sm["stk"], k=16)
            stk_goal = sm["stk"].n
            for pr in range(8):
                ta, tb = 2 * pr, 2 * pr + 1
                slot = pairs[0]
                if slab == 0 and pr < 2:
                    P.wait("y", sm["pTsp"], pTsp_total)
                if slab > 0 or pr >= 1:
                    P.wait("y", sm["psB"], sm["psB"].n)
                src3 = pT_scr[pr, :, :, :].rearrange("k p t -> p k t")
                dst3 = slot.rearrange("p (k t) -> p k t", t=256)
                P.op("y", lambda y, dst3=dst3, src3=src3: y.dma_start(out=dst3, in_=src3),
                     inc=sm["pairld"], k=16)
                if prdbg is not None and slab == 0 and pr == 1:
                    P.op("y", lambda y, slot=slot: y.dma_start(out=prdbg, in_=slot[:, :]),
                         inc=sm["pairld"], k=16)
                    sm["misc"].n += 0
                pld_goal = sm["pairld"].n
                bs = 0
                P.wait("t", sm["stk"], stk_goal)
                P.wait("t", sm["pairld"], pld_goal)
                if drB_hist[pr % 2] > 0:
                    P.wait("t", sm["drB"], drB_hist[pr % 2])
                for k in range(MT):
                    for j, off in enumerate((0, 128)):
                        fn = (lambda t, bs=bs, j=j, slot=slot, k=k, off=off:
                              t.matmul(pb[bs + j][:, :],
                                       slot[:, k * 256 + off:k * 256 + off + 128],
                                       wd_stack[:, k * 512:(k + 1) * 512],
                                       start=(k == 0), stop=(k == MT - 1),
                                       skip_group_check=True))
                        P.op("t", fn)
                pe_settle(6, sm["psB"])
                P.wait("s", sm["psB"], sm["psB"].n)
                for j, tt in enumerate((ta, tb)):
                    yslot = y1_stg[:, j * 512:(j + 1) * 512]
                    if slab > 0 or pr >= 2:
                        P.wait("s", sm["y1sp"], sm["y1sp"].n - 16)
                    P.op("s", lambda s, yslot=yslot, bs=bs, j=j: s.copy(out=yslot, in_=pb[bs + j][:, :]),
                         inc=sm["drB"], k=1)
                    P.wait("s", sm["drB"], sm["drB"].n)
                    P.op("s", lambda s, yslot=yslot, tt=tt, slab=slab: s.dma_start(
                        out=y1_scr[tt * 128:(tt + 1) * 128, slab * 512:(slab + 1) * 512],
                        in_=yslot), inc=sm["y1sp"], k=16)
                drB_hist[pr % 2] = sm["drB"].n
    y1sp_total = sm["y1sp"].n

    def act_ln(src_tile, dump_tile, out_ap, base, out_scale_extra=None):
        s1 = smalls[:, base + 0:base + 1]
        m = smalls[:, base + 1:base + 2]
        ss = smalls[:, base + 2:base + 3]
        e2 = smalls[:, base + 3:base + 4]
        nmsq = smalls[:, base + 4:base + 5]
        negm = smalls[:, base + 5:base + 6]
        rec = smalls[:, base + 6:base + 7]
        rstd = smalls[:, base + 7:base + 8]
        eps_c = smalls[:, 16:17]
        P.op("s", lambda s: s.activation(out=dump_tile[:, :], in_=src_tile[:, :],
                                         func=AF.Identity, accum_out=s1))
        P.op("s", lambda s: s.mul(m, s1, 1.0 / D))
        P.op("s", lambda s: s.activation(out=dump_tile[:, :], in_=src_tile[:, :],
                                         func=AF.Square, accum_out=ss))
        P.op("s", lambda s: s.activation(out=nmsq, in_=m, func=AF.Square))
        P.op("s", lambda s: s.activation(out=nmsq, in_=nmsq, func=AF.Identity,
                                         scale=-1.0, bias=eps_c))
        P.op("s", lambda s: s.activation(out=e2, in_=ss, func=AF.Identity,
                                         scale=1.0 / D, bias=nmsq))
        P.op("s", lambda s: s.activation(out=rec, in_=e2, func=AF.Ln))
        P.op("s", lambda s: s.activation(out=rstd, in_=rec, func=AF.Exp, scale=-0.5))
        P.op("s", lambda s: s.mul(negm, m, -1.0), inc=sm["bisv"], k=1)
        P.wait("s", sm["bisv"], sm["bisv"].n)
        P.op("s", lambda s: s.activation(out=dump_tile[:, :], in_=src_tile[:, :],
                                         func=AF.Identity, bias=negm))
        P.op("s", lambda s: s.activation(out=out_ap, in_=dump_tile[:, :],
                                         func=AF.Identity, scale=rstd))

    if "b3" in phases:
        for tau in range(NTT):
            P.wait("y", sm["y1sp"], y1sp_total)
            if tau > 0:
                P.wait("y", sm["xn2sp"], sm["xn2sp"].n)
            P.op("y", lambda y, tau=tau: y.dma_start(out=y1_tile[:, :],
                                                     in_=y1_scr[tau * 128:(tau + 1) * 128, :]),
                 inc=sm["y1rd"], k=16)
            P.op("y", lambda y, tau=tau: y.dma_start(out=x_t2[:, :],
                                                     in_=x_in[tau * 128:(tau + 1) * 128, :]),
                 inc=sm["y1rd"], k=16)
            P.wait("s", sm["y1rd"], sm["y1rd"].n)
            act_ln(y1_tile, ln_dump, ln_tile[:, :], 0)
            P.op("s", lambda s: s.mul(smalls[:, 31:32], smalls[:, 31:32], 1.0),
                 inc=sm["b3v"], k=1)
            P.wait("v", sm["b3v"], sm["b3v"].n)
            P.op("v", lambda v: v.tensor_add(xp_tile[:, :], x_t2[:, :], ln_tile[:, :]))
            P.op("v", lambda v: v.tensor_copy(out=smalls[:, 29:30], in_=smalls[:, 29:30]),
                 inc=sm["b3r"], k=1)
            P.wait("y", sm["b3r"], sm["b3r"].n)
            P.op("y", lambda y, tau=tau: y.dma_start(out=xp_scr[tau * 128:(tau + 1) * 128, :],
                                                     in_=xp_tile[:, :]), inc=sm["xpsp"], k=16)
            P.wait("s", sm["b3r"], sm["b3r"].n)
            act_ln(xp_tile, ln_dump, xn2_tile[:, :], 8)
            P.op("s", lambda s: s.mul(smalls[:, 31:32], smalls[:, 31:32], 1.0),
                 inc=sm["b3v"], k=1)
            P.wait("y", sm["b3v"], sm["b3v"].n)
            P.op("y", lambda y, tau=tau: y.dma_start(out=xn2_scr[tau * 128:(tau + 1) * 128, :],
                                                     in_=xn2_tile[:, :]), inc=sm["xn2sp"], k=16)
    xpsp_total = sm["xpsp"].n
    xn2sp_total = sm["xn2sp"].n

    def emit_final(tau):
        P.wait("y", sm["y2sp"], sm["y2sp"].n)
        P.wait("y", sm["xpsp"], xpsp_total)
        if tau > 0:
            P.wait("y", sm["outsp"], sm["outsp"].n)
        P.op("y", lambda y, tau=tau: y.dma_start(out=y2_tile[:, :],
                                                 in_=y2_scr[tau * 128:(tau + 1) * 128, :]),
             inc=sm["y2rd"], k=16)
        P.op("y", lambda y, tau=tau: y.dma_start(out=xp2_tile[:, :],
                                                 in_=xp_scr[tau * 128:(tau + 1) * 128, :]),
             inc=sm["y2rd"], k=16)
        P.wait("s", sm["y2rd"], sm["y2rd"].n)
        act_ln(y2_tile, ln_dump, out_tile[:, :], 0)
        P.op("s", lambda s: s.mul(smalls[:, 31:32], smalls[:, 31:32], 1.0), inc=sm["fv"], k=1)
        P.wait("v", sm["fv"], sm["fv"].n)
        P.op("v", lambda v: v.tensor_add(y2_tile[:, :], xp2_tile[:, :], out_tile[:, :]))
        P.op("v", lambda v: v.tensor_copy(out=smalls[:, 29:30], in_=smalls[:, 29:30]),
             inc=sm["fr"], k=1)
        P.wait("y", sm["fr"], sm["fr"].n)
        P.op("y", lambda y, tau=tau: y.dma_start(out=out_hbm[tau * 128:(tau + 1) * 128, :],
                                                 in_=y2_tile[:, :]), inc=sm["outsp"], k=16)


    if "c" in phases:
        P.wait("v", sm["misc"], 64)
        P.op("v", lambda v: v.tensor_copy(out=ssi_f[:, :], in_=ssi_i[:, :]), inc=sm["lora"], k=1)
        P.wait("v", sm["lora"], sm["lora"].n)
        P.op("v", lambda v: v.tensor_scalar(out=gw[:, 4:8], in0=ssi_f[:, 0:4], scalar1=0.5,
                                            scalar2=None, op0=OP.is_gt), inc=sm["lora"], k=1)
        P.wait("v", sm["lora"], sm["lora"].n)
        P.op("v", lambda v: v.tensor_scalar(out=gw[:, 8:12], in0=gw[:, 4:8], scalar1=-1.0,
                                            scalar2=1.0, op0=OP.mult, op1=OP.add),
             inc=sm["lora"], k=1)
        P.wait("v", sm["lora"], sm["lora"].n)
        P.op("v", lambda v: v.tensor_mul(gw[:, 0:1], gw[:, 6:7], gw[:, 9:10]), inc=sm["lora"], k=1)
        P.wait("v", sm["lora"], sm["lora"].n)
        P.op("v", lambda v: v.tensor_mul(gw[:, 1:2], gw[:, 10:11], gw[:, 7:8]), inc=sm["lora"], k=1)
        P.wait("v", sm["lora"], sm["lora"].n)
        P.op("v", lambda v: v.tensor_mul(gw[:, 2:3], gw[:, 10:11], gw[:, 11:12]), inc=sm["lora"], k=1)
        P.wait("v", sm["lora"], sm["lora"].n)
        P.op("v", lambda v: v.memset(ones48[:, :], 1.0), inc=sm["lora"], k=1)
        P.wait("t", sm["lora"], sm["lora"].n)
        P.op("t", lambda t: t.matmul(pb[6][0:48, 0:3], ones48[:, :], gw[0:1, 0:3],
                                     start=True, stop=True))
        pe_settle(7, sm["tmm"])
        P.wait("v", sm["tmm"], sm["tmm"].n)
        P.op("v", lambda v: v.tensor_mul(tmp48[:, :], pb[6][0:48, 0:3], emat_f[:, :]),
             inc=sm["lora"], k=1)
        P.wait("s", sm["lora"], sm["lora"].n)
        P.op("s", lambda s: s.activation(out=d48dmp[:, :], in_=tmp48[:, :], func=AF.Identity,
                                         accum_out=d48[0:48, 0:1]))
        P.op("s", lambda s: s.mul(smalls[:, 31:32], smalls[:, 31:32], 1.0), inc=sm["lora"], k=1)
        P.wait("y", sm["y1sp"], y1sp_total)
        P.op("y", lambda y: y.dma_start(out=bias_row[:, :], in_=bias_in), inc=sm["misc"], k=16)
        P.wait("t", sm["misc"], sm["misc"].n)
        for j in range(8):
            P.op("t", lambda t, j=j: t.matmul(pb[4 + (j % 2)][:, :], onesf[:, :],
                                              bias_row[0:1, j * 512:(j + 1) * 512],
                                              start=True, stop=True))
            pe_settle(6, sm["tmm"])
            P.wait("s", sm["tmm"], sm["tmm"].n)
            P.op("s", lambda s, j=j: s.copy(out=bias_b[:, j * 512:(j + 1) * 512],
                                            in_=pb[4 + (j % 2)][:, :]), inc=sm["drC"], k=1)
            if j >= 1:
                P.wait("t", sm["drC"], sm["drC"].n - 1)
        bias_drC_total = sm["drC"].n
        P.wait("g", sm["y1sp"], y1sp_total)
        P.op("g", lambda g: g.dma_start(out=AcatT[:, :], in_=lA_in), inc=sm["castl"], k=16)
        P.op("g", lambda g: g.dma_start(out=BcatT[:, :], in_=lB_in), inc=sm["castl"], k=16)
        P.wait("y", sm["castl"], sm["castl"].n)
        lora_ld = sm["lora"].n

        psC_hist = {}
        for q in range(4):
            P.wait("y", sm["castf"], 16 * (q + 1))
            if q > 0:
                P.wait("y", sm["psC"], sm["psC"].n)
            else:
                if "b3" in phases:
                    P.wait("y", sm["xn2sp"], xn2sp_total)
                elif "b2" in phases:
                    P.wait("y", sm["psB"], sm["psB"].n)
            for k in range(KT):
                for oc in range(2):
                    P.op("y", lambda y, k=k, oc=oc, q=q: y.dma_start_transpose(
                        wf_stack[:, k * 1024 + oc * 512:k * 1024 + (oc + 1) * 512],
                        wf_bf[q * 1024 + oc * 512:q * 1024 + (oc + 1) * 512,
                              k * 128:(k + 1) * 128]), inc=sm["wfstk"], k=16)
            wf_goal = sm["wfstk"].n
            for h in range(2):
                P.wait("y", sm["xn2sp"], xn2sp_total)
                if q > 0 or h > 0:
                    P.wait("y", sm["psC"], sm["psC"].n)
                for k in range(KT):
                    P.op("y", lambda y, k=k, h=h: y.dma_start_transpose(
                        xn2T[:, k * 1024:(k + 1) * 1024],
                        xn2_scr[h * HALF:(h + 1) * HALF, k * 128:(k + 1) * 128]),
                        inc=sm["x2t"], k=16)
                x2t_goal = sm["x2t"].n
                if q == 0:
                    P.wait("t", sm["x2t"], x2t_goal)
                    P.wait("t", sm["castl"], sm["castl"].n)
                    P.wait("t", sm["lora"], lora_ld)
                    P.wait("t", sm["drC"], bias_drC_total)
                    for tc in range(2):
                        for k in range(KT):
                            P.op("t", lambda t, k=k, tc=tc: t.matmul(
                                pb[4][0:48, :], AcatT[:, k * 48:(k + 1) * 48],
                                xn2T[:, k * 1024 + tc * 512:k * 1024 + (tc + 1) * 512],
                                start=(k == 0), stop=(k == KT - 1),
                                skip_group_check=True))
                        pe_settle(5, sm["psL"])
                        P.wait("v", sm["psL"], sm["psL"].n)
                        tcg = h * 2 + tc
                        P.op("v", lambda v, tcg=tcg: v.tensor_scalar(
                            out=dmidT[0:48, tcg * 512:(tcg + 1) * 512], in0=pb[4][0:48, 0:512],
                            scalar1=d48[0:48, 0:1], scalar2=None, op0=OP.mult),
                            inc=sm["lora"], k=1)
                        P.wait("t", sm["lora"], sm["lora"].n)
                for tl in range(8):
                    tau = h * 8 + tl
                    bs = (tl % 2) * 2
                    P.wait("t", sm["wfstk"], wf_goal)
                    P.wait("t", sm["x2t"], x2t_goal)
                    key = bs
                    if key in psC_hist:
                        P.wait("t", sm["drC"], psC_hist[key])
                    for k in range(KT):
                        for oc in range(2):
                            fn = (lambda t, bs=bs, oc=oc, k=k, tl=tl:
                                  t.matmul(pb[bs + oc][:, :],
                                           xn2T[:, k * 1024 + tl * 128:k * 1024 + tl * 128 + 128],
                                           wf_stack[:, k * 1024 + oc * 512:k * 1024 + (oc + 1) * 512],
                                           start=(k == 0), stop=False,
                                           skip_group_check=True))
                            P.op("t", fn)
                    for oc in range(2):
                        fn = (lambda t, bs=bs, oc=oc, tau=tau, q=q:
                              t.matmul(pb[bs + oc][:, :],
                                       dmidT[0:48, tau * 128:(tau + 1) * 128],
                                       BcatT[0:48, q * 1024 + oc * 512:q * 1024 + (oc + 1) * 512],
                                       start=False, stop=True, skip_group_check=True))
                        P.op("t", fn)
                    pe_settle(5, sm["psC"], k=2)
                    P.wait("v", sm["psC"], sm["psC"].n)
                    for oc in range(2):
                        if sm["y2sp"].n >= 32:
                            P.wait("v", sm["y2sp"], sm["y2sp"].n - 16)
                        P.op("v", lambda v, oc=oc, bs=bs, q=q: v.tensor_tensor(
                            out=y2_stg[:, oc * 512:(oc + 1) * 512], in0=pb[bs + oc][:, :],
                            in1=bias_b[:, q * 1024 + oc * 512:q * 1024 + (oc + 1) * 512],
                            op=OP.add), inc=sm["drC"], k=1)
                        P.wait("y", sm["drC"], sm["drC"].n)
                        P.op("y", lambda y, oc=oc, tau=tau, q=q: y.dma_start(
                            out=y2_scr[tau * 128:(tau + 1) * 128,
                                       q * 1024 + oc * 512:q * 1024 + (oc + 1) * 512],
                            in_=y2_stg[:, oc * 512:(oc + 1) * 512]), inc=sm["y2sp"], k=16)
                    psC_hist[bs] = sm["drC"].n

    if "f" in phases:
        for tau in range(NTT):
            emit_final(tau)
        P.wait("y", sm["outsp"], sm["outsp"].n)

    def emit_debug_and_finish():
        fin = sm["outsp"]
        for s_ in ("xnsp", "qspill", "aspill", "alosp", "pTsp", "y1sp", "xpsp", "xn2sp", "y2sp"):
            if sm[s_].n > 0:
                P.wait("g", sm[s_], sm[s_].n)
        for name, (out_ap, src_ap) in dbg.items():
            rows = src_ap.shape[0]
            step = 2048
            for r in range(0, rows, step):
                rr = min(step, rows - r)
                P.op("g", lambda g, out_ap=out_ap, src_ap=src_ap, r=r, rr=rr:
                     g.dma_start(out=out_ap[r:r + rr, :], in_=src_ap[r:r + rr, :]),
                     inc=fin, k=16)
        P.wait("g", fin, fin.n)

    if dbg or tdbg is not None or e2dbg is not None or tmdbg is not None:
        emit_debug_and_finish()
        if tmdbg is not None:
            P.wait("y", sm["pTsp"], sm["pTsp"].n)
            P.op("y", lambda y: y.dma_start(out=tmdbg, in_=t_mat_f[:, :]), inc=sm["outsp"], k=16)
            P.wait("y", sm["outsp"], sm["outsp"].n)
        if tdbg is not None:
            P.wait("g", sm["bis"], sm["bis"].n)
            P.op("g", lambda g: g.dma_start(out=tdbg[:, 0:NTT], in_=t_tile[:, :]),
                 inc=sm["outsp"], k=16)
            P.op("g", lambda g: g.dma_start(out=tdbg[:, 16:32], in_=smalls[:, 16:32]),
                 inc=sm["outsp"], k=16)
            P.op("g", lambda g: g.dma_start(out=tdbg[:, 96:128], in_=t_mat_f[:, 0:32]),
                 inc=sm["outsp"], k=16)
            P.op("g", lambda g: g.dma_start(out=tdbg[:, 48:112], in_=ident_sb[:, 0:64]),
                 inc=sm["outsp"], k=16)
            P.op("g", lambda g: g.dma_start(out=tdbg[:, 112:113], in_=t_stage[:, :],
                                            allow_slow_non_contiguous=True),
                 inc=sm["outsp"], k=16)
            P.op("g", lambda g: g.dma_start(out=tdbg[0:1, 113:127], in_=trow[0:1, 0:14]),
                 inc=sm["outsp"], k=16)
            P.op("g", lambda g: g.dma_start(out=tdbg[1:2, 113:127], in_=onesf[0:1, 0:14]),
                 inc=sm["outsp"], k=16)
            P.wait("v", sm["tmat"], sm["tmat"].n)
            P.op("v", lambda v: v.tensor_copy(out=trow[0:1, 0:64], in_=pb[7][0:1, 0:64]),
                 inc=sm["fv"], k=1)
            P.wait("g", sm["fv"], sm["fv"].n)
            P.op("g", lambda g: g.dma_start(out=tdbg[2:3, 64:128], in_=trow[0:1, 0:64]),
                 inc=sm["outsp"], k=16)
            P.wait("g", sm["outsp"], sm["outsp"].n)
        if e2dbg is not None:
            P.wait("g", sm["xnsp"], sm["xnsp"].n)
            P.op("g", lambda g: g.dma_start(out=e2dbg[:, 0:NTT], in_=smalls[:, 32:32 + NTT]),
                 inc=sm["outsp"], k=16)
            P.op("g", lambda g: g.dma_start(out=e2dbg[:, NTT:2 * NTT], in_=smalls[:, 64:64 + NTT]),
                 inc=sm["outsp"], k=16)
            P.wait("g", sm["outsp"], sm["outsp"].n)

    with nc.Block() as blk0:
        @blk0.sync
        def _(y):
            for s_ in sm.values():
                y.sem_clear(s_.h)
    with nc.Block() as block:
        @block.tensor
        def _(t):
            for f in P.q["t"]:
                f(t)

        @block.vector
        def _(v):
            for f in P.q["v"]:
                f(v)

        @block.scalar
        def _(s):
            for f in P.q["s"]:
                f(s)

        @block.sync
        def _(y):
            for f in P.q["y"]:
                f(y)

        @block.gpsimd
        def _(g):
            for f in P.q["g"]:
                f(g)

    nc._keepalive = es
    return nc


def _consts():
    emat = np.zeros((48, 3), dtype=np.float32)
    for r in range(48):
        emat[r, r // 16] = 1.0
    ident = np.eye(128, dtype=np.float32)
    return emat, ident


def make_inmap(core, full):
    emat, ident = _consts()
    x = full["L0_attn_x"].reshape(16384, 4096)
    return {
        "x": np.ascontiguousarray(x[core * T:(core + 1) * T]),
        "w_gate": full["L0_attn_w_gate"],
        "w_up": full["L0_attn_w_up"],
        "w_down": full["L0_attn_w_down"],
        "w_ffn": full["L0_ffn_weight"],
        "bias": full["L0_ffn_bias"].reshape(1, 4096),
        "lora_A": np.ascontiguousarray(
            np.transpose(np.transpose(full["L0_ffn_lora_A"].reshape(48, 4096))
                         .reshape(32, 128, 48), (1, 0, 2)).reshape(128, 32 * 48)),
        "lora_B": np.ascontiguousarray(
            np.transpose(full["L0_ffn_lora_B"], (0, 2, 1)).reshape(48, 4096)),
        "ssi": np.asarray(full["L0_ffn_lora_start_indices"], dtype=np.int32).reshape(1, 4),
        "emat": emat,
        "ident": ident,
    }


def kernel(**inputs):
    nc = build_nc()
    in_maps = [make_inmap(c, inputs) for c in range(8)]
    res = run_bass_kernel_spmd(nc, in_maps, core_ids=list(range(8)))
    out = np.concatenate([res.results[c]["out"] for c in range(8)], axis=0)
    return out.reshape(1, 16384, 4096)
